# version 11
# baseline (speedup 1.0000x reference)
import numpy as np
import ml_dtypes
from contextlib import ExitStack

import concourse.bass as bass
import concourse.mybir as mybir
from concourse.bass_utils import run_bass_kernel_spmd

F32 = mybir.dt.float32
BF16 = mybir.dt.bfloat16
AF = mybir.ActivationFunctionType
ALU = mybir.AluOpType
AXX = mybir.AxisListType.X
BF = ml_dtypes.bfloat16

B, S, D, H, DK = 4, 4096, 1024, 16, 64
NG, GT = 8, 512
NT = S // 128
NHP = 4

UNITS = [(g, hp, kt) for g in range(NG) for hp in range(NHP) for kt in range(4 * g + 4)]
NU = len(UNITS)
ULAST = {}
for i, (g, hp, kt) in enumerate(UNITS):
    ULAST[(g, hp)] = i
UOFF = [0] * (NG + 1)
for g in range(NG):
    UOFF[g + 1] = UOFF[g] + 4 * (4 * g + 4)

NWC = 28
NCONST = 12

_DRAIN_OPS = {
    "tensor_copy", "tensor_tensor", "tensor_add", "tensor_sub", "tensor_mul",
    "tensor_max", "tensor_scalar", "tensor_single_scalar", "tensor_scalar_add",
    "tensor_scalar_sub", "tensor_scalar_mul", "tensor_scalar_max", "tensor_scalar_min",
    "tensor_reduce", "reduce_max", "reduce_sum", "scalar_tensor_tensor",
    "bn_stats", "bn_aggr", "reciprocal", "memset", "memzero", "activation",
    "add", "mul", "copy", "square", "sqrt", "select", "transpose",
}
_DRAINS_ON = [True]


def _ap_tensor_names(args):
    names = []
    for a in args:
        t = getattr(a, "tensor", None)
        if t is not None:
            names.append(getattr(t, "name", str(t)))
    return names


class _Draining:

    def __init__(self, eng):
        self._e = eng
        self._written = set()
        self._read = set()

    def __getattr__(self, name):
        attr = getattr(self._e, name)
        if name in _DRAIN_OPS and _DRAINS_ON[0]:
            def wrapped(*a, **k):
                names = _ap_tensor_names(list(a) + list(k.values()))
                out = names[0] if names else None
                need = any(n in self._written for n in names) or (out in self._read)
                if need:
                    self._e.drain()
                    self._written.clear()
                    self._read.clear()
                if out is not None:
                    self._written.add(out)
                for n in names[1:]:
                    self._read.add(n)
                return attr(*a, **k)
            return wrapped
        return attr


def _build(drains=True):
    _DRAINS_ON[0] = drains
    nc = bass.Bass("TRN2", target_bir_lowering=False, debug=False)

    def din(name, shape, dt):
        return nc.dram_tensor(name, shape, dt, kind="ExternalInput").ap()

    x_b = din("x_b", [S, D], F32)
    x_own = din("x_own", [S // 2, D], F32)
    wq_d = din("wq_d", [8, 128, 512], F32)
    wk_d = din("wk_d", [8, 128, 512], F32)
    wv_d = din("wv_d", [8, 128, 512], F32)
    wo_d = din("wo_d", [4, 128, 1024], F32)
    bq_d = din("bq_d", [128, 4], F32)
    bk_d = din("bk_d", [128, 4], F32)
    bv_d = din("bv_d", [128, 512], F32)
    t1_d = din("t1_d", [128, S], F32)
    t2_d = din("t2_d", [128, S], F32)
    msk_d = din("msk_d", [128, 128], BF16)
    idn_d = din("idn_d", [128, 128], BF16)
    oblk_d = din("oblk_d", [128, 64], BF16)
    pvo_d = din("pvo_d", [128, 32], BF16)
    g_d = din("g_d", [128, D], F32)
    bb_d = din("bb_d", [128, D], F32)
    bo_d = din("bo_d", [128, D], F32)
    a1_d = din("a1_d", [128, 128], BF16)
    a2_d = din("a2_d", [128, 128], BF16)
    out_d = nc.dram_tensor("out", [S // 2, D], F32, kind="ExternalOutput").ap()
    cc_in = nc.dram_tensor("cc_in", [S, D], F32)
    cc_out = nc.dram_tensor("cc_out", [S // 2, D], F32)
    wqh_d = nc.dram_tensor("wqh_d", [4, 8, 128, 128], BF16)
    wkh_d = nc.dram_tensor("wkh_d", [4, 8, 128, 128], BF16)

    ctx = ExitStack()
    sem = {}
    for name in ["d_ws0", "d_wc", "d_wb", "d_x0", "d_x1", "d_t0", "d_t1",
                 "d_tr0", "d_tr1", "d_wq0", "d_wq1",
                 "d_pa0", "d_pa1", "d_a3", "d_out",
                 "s_cast", "s_w", "s_r0", "s_ln",
                 "s_qmm", "s_kmm", "s_qe", "s_ke",
                 "s_mq", "s_mk", "s_cq", "s_ck", "s_qhe", "s_khe",
                 "s_rq", "s_rk", "s_nmm", "s_nmax", "s_vmm", "s_vev",
                 "s_b2", "s_bexp", "s_bias",
                 "s_exp", "s_qk2", "s_pv", "s_rr", "s_onorm",
                 "s_womm", "s_paev", "s_cc",
                 "s_t3", "s_x13", "s_w3", "s_r3", "s_z3", "s_zg3", "s_o3"]:
        sem[name] = ctx.enter_context(nc.semaphore(name))

    gsb = lambda n, sh, dt: ctx.enter_context(nc.sbuf_tensor(n, sh, dt))
    idn_sb = gsb("idn_sb", [128, 128], BF16)
    oblk_sb = gsb("oblk_sb", [128, 64], BF16)
    pvo_sb = gsb("pvo_sb", [128, 32], BF16)
    a1_sb = gsb("a1_sb", [128, 128], BF16)
    a2_sb = gsb("a2_sb", [128, 128], BF16)
    bias_sb = gsb("bias_sb", [128, 2, 4], F32)
    nmaxq_sb = gsb("nmaxq_sb", [64, 4], F32)
    nmaxk_sb = gsb("nmaxk_sb", [64, 4], F32)
    nk2_sb = gsb("nk2_sb", [64, 4], F32)
    bsc_sb = gsb("bsc_sb", [64, 12], F32)
    bq_sb = gsb("bq_sb", [128, 4], F32)
    bk_sb = gsb("bk_sb", [128, 4], F32)
    bv_sb = gsb("bv_sb", [128, 512], F32)
    khat = gsb("khat", [128, 4, S], BF16)
    qhat = gsb("qhat", [128, 2, 4, GT], BF16)
    vau = gsb("vau", [128, 4, NT, 128], BF16)
    onog = gsb("onog", [128, 4, GT], BF16)
    wo_bf = gsb("wo_bf", [128, 4, 1024], BF16)
    pa_sb = gsb("pa_sb", [128, 2, 1024], F32)
    p_sb = gsb("p_sb", [128, 2, 1024], BF16)
    msk_sb = gsb("msk_sb", [128, 128], BF16)
    rr_sb = gsb("rr_sb", [32, 2, GT], F32)
    stg_sb = gsb("stg_sb", [128, 1024], F32)
    wcast_sb = gsb("wcast_sb", [128, 512], BF16)
    wvr_sb = gsb("wvr_sb", [128, 8, 512], BF16)
    wqr_sb = gsb("wqr_sb", [128, 2, 8, 128], BF16)
    wkr_sb = gsb("wkr_sb", [128, 2, 8, 128], BF16)
    t1_sb = gsb("t1_sb", [128, 2, GT], F32)
    t2_sb = gsb("t2_sb", [128, 2, GT], F32)
    x_sb = gsb("x_sb", [128, 2, 1024], F32)
    ubf_sb = gsb("ubf_sb", [128, 2, 1024], BF16)
    utg_sb = gsb("utg_sb", [128, 8, GT], BF16)
    tq_sb = gsb("tq_sb", [128, GT], F32)
    tk_sb = gsb("tk_sb", [128, GT], F32)
    m1q_sb = gsb("m1q_sb", [128, GT], BF16)
    m2q_sb = gsb("m2q_sb", [128, GT], BF16)
    m1k_sb = gsb("m1k_sb", [128, GT], BF16)
    m2k_sb = gsb("m2k_sb", [128, GT], BF16)
    sqq_sb = gsb("sqq_sb", [128, GT], BF16)
    sqk_sb = gsb("sqk_sb", [128, GT], BF16)
    st_sb = gsb("st_sb", [128, 12], F32)
    ag_sb = gsb("ag_sb", [128, 4], F32)
    sc_sb = gsb("sc_sb", [128, 16], F32)
    wln_sb = gsb("wln_sb", [128, 4], F32)
    r0_sb = gsb("r0_sb", [128, 4], F32)
    g_sb = gsb("g_sb", [128, D], F32)
    bbc_sb = gsb("bbc_sb", [128, D], F32)
    bo_sb = gsb("bo_sb", [128, D], F32)
    a3_sb = gsb("a3_sb", [128, 1024], F32)
    x3_sb = gsb("x3_sb", [128, 1024], F32)
    t3_sb = gsb("t3_sb", [128, 1024], F32)
    x13_sb = gsb("x13_sb", [128, 2, 1024], F32)
    z3_sb = gsb("z3_sb", [128, 1024], F32)
    zg3_sb = gsb("zg3_sb", [128, 1024], F32)
    o3_sb = gsb("o3_sb", [128, 1024], F32)
    st3_sb = gsb("st3_sb", [128, 12], F32)
    ag3_sb = gsb("ag3_sb", [128, 4], F32)
    sc3_sb = gsb("sc3_sb", [128, 16], F32)
    m3_sb = gsb("m3_sb", [128, 2], F32)
    w3_sb = gsb("w3_sb", [128, 2], F32)
    r3_sb = gsb("r3_sb", [128, 2], F32)

    s_pair = [ctx.enter_context(nc.psum_tensor(f"s_pair{i}", [128, 1024], F32)) for i in range(2)]
    o_psA = ctx.enter_context(nc.psum_tensor("o_psA", [128, 512], F32))
    o_psB = ctx.enter_context(nc.psum_tensor("o_psB", [128, 512], F32))
    p1a_ps = ctx.enter_context(nc.psum_tensor("p1a_ps", [128, 512], F32))
    p1b_ps = ctx.enter_context(nc.psum_tensor("p1b_ps", [128, 512], F32))

    block = ctx.enter_context(nc.Block())
    CHUNKS = ([("wq", f) for f in range(8)] + [("wk", f) for f in range(8)]
              + [("wv", f) for f in range(8)] + [("wo", f) for f in range(4)])

    @block.sync
    def _(sync):
        nwb = 0
        for i, (w, f) in enumerate(CHUNKS):
            if i >= 1:
                sync.wait_ge(sem["s_cast"], i)
            if i >= 1 and CHUNKS[i - 1][0] in ("wq", "wk"):
                pw, pf = CHUNKS[i - 1]
                dstd = wqh_d if pw == "wq" else wkh_d
                for hp in range(4):
                    sync.dma_start(out=dstd.ap()[hp, pf, :, :],
                                   in_=wcast_sb[:, hp * 128:(hp + 1) * 128]).then_inc(sem["d_wb"], 16)
                    nwb += 1
            src = {"wq": wq_d, "wk": wk_d, "wv": wv_d, "wo": wo_d}[w][f]
            width = 1024 if w == "wo" else 512
            sync.dma_start(out=stg_sb[:, 0:width], in_=src).then_inc(sem["d_ws0"], 16)
        for src, dst in [(bq_d, bq_sb[:]), (bk_d, bk_sb[:]), (bv_d, bv_sb[:]),
                         (idn_d, idn_sb[:]), (oblk_d, oblk_sb[:]), (pvo_d, pvo_sb[:]),
                         (msk_d, msk_sb[:]),
                         (a1_d, a1_sb[:]), (a2_d, a2_sb[:]),
                         (g_d, g_sb[:]), (bb_d, bbc_sb[:]), (bo_d, bo_sb[:])]:
            sync.dma_start(out=dst, in_=src).then_inc(sem["d_wc"], 16)
        sync.wait_ge(sem["s_cast"], 16)
        sync.wait_ge(sem["d_wb"], 16 * 64)

        def prep_dmas(gp):
            for j in range(4):
                tt = 4 * gp + j
                if tt >= 2:
                    sync.wait_ge(sem["s_ln"], tt - 1)
                sync.dma_start(out=x_sb[:, tt % 2, :],
                               in_=x_b[tt * 128:(tt + 1) * 128, :]).then_inc(sem["d_x%d" % (tt % 2)], 16)
            if gp >= 2:
                sync.wait_ge(sem["s_mq"], 4 * (gp - 1))
                sync.wait_ge(sem["s_mk"], 4 * (gp - 1))
            sync.dma_start(out=t1_sb[:, gp % 2, :], in_=t1_d[:, gp * GT:(gp + 1) * GT]).then_inc(sem["d_t%d" % (gp % 2)], 16)
            sync.dma_start(out=t2_sb[:, gp % 2, :], in_=t2_d[:, gp * GT:(gp + 1) * GT]).then_inc(sem["d_t%d" % (gp % 2)], 16)
            for j in range(4):
                tt = 4 * gp + j
                sync.wait_ge(sem["s_ln"], tt + 1)
                if gp >= 1:
                    sync.wait_ge(sem["s_vmm"], 4 * gp)
                for cch in range(8):
                    sync.dma_start_transpose(utg_sb[:, cch, j * 128:(j + 1) * 128],
                                             ubf_sb[:, tt % 2, cch * 128:(cch + 1) * 128]).then_inc(sem["d_tr%d" % (tt % 2)], 16)
            for hp in range(4):
                idx = 4 * gp + hp
                if idx >= 2:
                    sync.wait_ge(sem["s_qmm"], idx - 1)
                    sync.wait_ge(sem["s_kmm"], idx - 1)
                sync.dma_start(out=wqr_sb[:, hp % 2, :, :],
                               in_=wqh_d.ap()[hp].rearrange("f p c -> p f c")).then_inc(sem["d_wq%d" % (hp % 2)], 16)
                sync.dma_start(out=wkr_sb[:, hp % 2, :, :],
                               in_=wkh_d.ap()[hp].rearrange("f p c -> p f c")).then_inc(sem["d_wq%d" % (hp % 2)], 16)

        prep_dmas(0)
        for g in range(NG):
            if g < 7:
                prep_dmas(g + 1)
            for j in range(4):
                tt = 4 * g + j
                sync.wait_ge(sem["s_paev"], tt + 1)
                sync.dma_start(out=cc_in.ap()[tt * 128:(tt + 1) * 128, :],
                               in_=pa_sb[:, tt % 2, :]).then_inc(sem["d_pa%d" % (tt % 2)], 16)
            if g >= 2:
                for t3 in (2 * (g - 2), 2 * (g - 2) + 1):
                    sync.wait_ge(sem["s_cc"], t3 // 2 + 1)
                    sync.wait_ge(sem["s_t3"], t3)
                    sync.dma_start(out=a3_sb[:],
                                   in_=cc_out.ap()[t3 * 128:(t3 + 1) * 128, :]).then_inc(sem["d_a3"], 16)
                    sync.dma_start(out=x3_sb[:],
                                   in_=x_own[t3 * 128:(t3 + 1) * 128, :]).then_inc(sem["d_a3"], 16)
                for t3 in (2 * (g - 2), 2 * (g - 2) + 1):
                    sync.wait_ge(sem["s_o3"], t3 + 1)
                    sync.dma_start(out=out_d[t3 * 128:(t3 + 1) * 128, :],
                                   in_=o3_sb[:]).then_inc(sem["d_out"], 16)
        for gg in (6, 7):
            for t3 in (2 * gg, 2 * gg + 1):
                sync.wait_ge(sem["s_cc"], t3 // 2 + 1)
                sync.wait_ge(sem["s_t3"], t3)
                sync.dma_start(out=a3_sb[:],
                               in_=cc_out.ap()[t3 * 128:(t3 + 1) * 128, :]).then_inc(sem["d_a3"], 16)
                sync.dma_start(out=x3_sb[:],
                               in_=x_own[t3 * 128:(t3 + 1) * 128, :]).then_inc(sem["d_a3"], 16)
            for t3 in (2 * gg, 2 * gg + 1):
                sync.wait_ge(sem["s_o3"], t3 + 1)
                sync.dma_start(out=out_d[t3 * 128:(t3 + 1) * 128, :],
                               in_=o3_sb[:]).then_inc(sem["d_out"], 16)
        sync.wait_ge(sem["d_out"], 16 * 16)

    @block.vector
    def _(vector):
        vector = _Draining(vector)
        nwbk = 0
        for i, (w, f) in enumerate(CHUNKS):
            vector.wait_ge(sem["d_ws0"], 16 * (i + 1))
            if w in ("wq", "wk"):
                if nwbk >= 1:
                    vector.wait_ge(sem["d_wb"], 64 * nwbk)
                vector.tensor_copy(wcast_sb[:], stg_sb[:, 0:512]).then_inc(sem["s_cast"], 1)
                nwbk += 1
            elif w == "wv":
                vector.tensor_copy(wvr_sb[:, f, :], stg_sb[:, 0:512]).then_inc(sem["s_cast"], 1)
            else:
                vector.tensor_copy(wo_bf[:, f, :], stg_sb[:, 0:1024]).then_inc(sem["s_cast"], 1)
        vector.wait_ge(sem["d_wc"], 16 * NCONST)
        vector.memset(s_pair[0][:], 0.0)
        vector.memset(s_pair[1][:], 0.0)
        vector.memset(o_psA[64:96, :], 0.0)
        vector.memset(o_psB[64:96, :], 0.0).then_inc(sem["s_onorm"], 1)

        def ln_tile(gp, j):
            tt = 4 * gp + j
            vector.wait_ge(sem["d_x%d" % (tt % 2)], 16 * (tt // 2 + 1))
            if tt >= 2:
                vector.wait_ge(sem["d_tr%d" % (tt % 2)], 128 * (tt // 2))
            xt = x_sb[:, tt % 2, :]
            vector.bn_stats(st_sb[:, 0:6], xt[:, 0:512])
            vector.bn_stats(st_sb[:, 6:12], xt[:, 512:1024])
            vector.bn_aggr(ag_sb[:, 0:2], st_sb[:, 0:6])
            vector.bn_aggr(ag_sb[:, 2:4], st_sb[:, 6:12])
            m1, v1, m2, v2 = (ag_sb[:, i:i + 1] for i in range(4))
            vector.tensor_add(sc_sb[:, 0:1], m1, m2)
            vector.tensor_scalar_mul(sc_sb[:, 1:2], sc_sb[:, 0:1], 0.5)
            vector.tensor_sub(sc_sb[:, 2:3], m1, m2)
            vector.tensor_mul(sc_sb[:, 3:4], sc_sb[:, 2:3], sc_sb[:, 2:3])
            vector.tensor_add(sc_sb[:, 4:5], v1, v2)
            vector.tensor_scalar_mul(sc_sb[:, 5:6], sc_sb[:, 4:5], 0.5)
            vector.scalar_tensor_tensor(sc_sb[:, 6:7], sc_sb[:, 3:4], 0.25,
                                        sc_sb[:, 5:6], ALU.mult, ALU.add)
            vector.tensor_scalar_add(wln_sb[:, j:j + 1], sc_sb[:, 6:7], 1e-5).then_inc(sem["s_w"], 1)
            vector.wait_ge(sem["s_r0"], tt + 1)
            r0 = r0_sb[:, j:j + 1]
            vector.tensor_mul(sc_sb[:, 7:8], r0, r0)
            vector.tensor_mul(sc_sb[:, 8:9], sc_sb[:, 7:8], wln_sb[:, j:j + 1])
            vector.tensor_scalar(sc_sb[:, 9:10], sc_sb[:, 8:9], -0.5, 1.5, ALU.mult, ALU.add)
            vector.tensor_mul(sc_sb[:, 10:11], r0, sc_sb[:, 9:10])
            vector.tensor_scalar(ubf_sb[:, tt % 2, :], xt, sc_sb[:, 1:2],
                                 sc_sb[:, 10:11], ALU.subtract, ALU.mult).then_inc(sem["s_ln"], 1)

        def onorm_block(g, hp):
            vector.wait_ge(sem["s_rr"], 4 * g + hp + 1)
            vector.tensor_mul(onog[0:32, hp, :], o_psA[0:32, :], rr_sb[:, 0, :])
            vector.tensor_mul(onog[32:64, hp, :], o_psA[32:64, :], rr_sb[:, 0, :])
            vector.tensor_mul(onog[64:96, hp, :], o_psB[0:32, :], rr_sb[:, 1, :])
            vector.tensor_mul(onog[96:128, hp, :], o_psB[32:64, :], rr_sb[:, 1, :])
            vector.memset(o_psA[64:96, :], 0.0)
            vector.memset(o_psB[64:96, :], 0.0).then_inc(sem["s_onorm"], 1)

        def prep_chain(gp):
            for hp in range(4):
                idx = 4 * gp + hp
                vector.wait_ge(sem["s_qe"], idx + 1)
                vector.wait_ge(sem["s_cq"], idx)
                vector.wait_ge(sem["d_t%d" % (gp % 2)], 32 * (gp // 2 + 1))
                t1g, t2g = t1_sb[:, gp % 2, :], t2_sb[:, gp % 2, :]
                vector.scalar_tensor_tensor(m1q_sb[:], tq_sb[:], 0.125, t1g, ALU.mult, ALU.mult)
                vector.scalar_tensor_tensor(m2q_sb[:], tq_sb[:], 0.125, t2g, ALU.mult, ALU.mult).then_inc(sem["s_mq"], 1)
                vector.wait_ge(sem["s_ke"], idx + 1)
                vector.wait_ge(sem["s_ck"], idx)
                vector.tensor_mul(m1k_sb[:], tk_sb[:], t1g)
                vector.tensor_mul(m2k_sb[:], tk_sb[:], t2g).then_inc(sem["s_mk"], 1)
                vector.wait_ge(sem["s_qhe"], idx + 1)
                vector.tensor_mul(sqq_sb[:], qhat[:, gp % 2, hp, :], qhat[:, gp % 2, hp, :]).then_inc(sem["s_rq"], 1)
                vector.wait_ge(sem["s_khe"], idx + 1)
                if idx >= 1:
                    vector.wait_ge(sem["s_nmm"], 2 * idx)
                vector.tensor_mul(sqk_sb[:], khat[:, hp, gp * GT:(gp + 1) * GT],
                                  khat[:, hp, gp * GT:(gp + 1) * GT]).then_inc(sem["s_rk"], 1)
                vector.wait_ge(sem["s_nmm"], 2 * idx + 1)
                vector.reduce_max(nmaxq_sb[:, hp:hp + 1], p1a_ps[0:64, :], AXX).then_inc(sem["s_nmax"], 1)
                vector.wait_ge(sem["s_nmm"], 2 * idx + 2)
                vector.reduce_max(nmaxk_sb[:, hp:hp + 1], p1b_ps[0:64, :], AXX).then_inc(sem["s_nmax"], 1)
            if gp == 0:
                vector.tensor_copy(nk2_sb[:], nmaxk_sb[:])
            else:
                vector.tensor_max(nk2_sb[:], nk2_sb[:], nmaxk_sb[:])
            vector.tensor_mul(bsc_sb[:, 0:4], nmaxq_sb[:], nk2_sb[:]).then_inc(sem["s_b2"], 1)
            vector.wait_ge(sem["s_bexp"], gp + 1)
            vector.tensor_scalar_mul(bsc_sb[:, 8:12], bsc_sb[:, 4:8], -1.0)
            vector.tensor_copy(p1b_ps[32:64, 0:4], bsc_sb[32:64, 8:12])
            vector.tensor_tensor(bias_sb[0:32, gp % 2, :], bsc_sb[0:32, 8:12],
                                 p1b_ps[32:64, 0:4], ALU.min)
            vector.tensor_copy(bias_sb[32:64, gp % 2, :], bias_sb[0:32, gp % 2, :])
            vector.tensor_copy(bias_sb[64:128, gp % 2, :], bias_sb[0:64, gp % 2, :]).then_inc(sem["s_bias"], 1)
            for j in range(4):
                tt = 4 * gp + j
                vector.wait_ge(sem["s_vmm"], tt + 1)
                vp = p1a_ps if j % 2 == 0 else p1b_ps
                vector.tensor_add(
                    vau[:, :, tt, :],
                    vp[:].rearrange("p (h c) -> p h c", h=4),
                    bv_sb[:].rearrange("p (h c) -> p h c", h=4),
                ).then_inc(sem["s_vev"], 1)

        def pa_evacs(g):
            for j in range(4):
                tt = 4 * g + j
                vector.wait_ge(sem["s_womm"], tt + 1)
                if tt >= 2:
                    vector.wait_ge(sem["d_pa%d" % (tt % 2)], 16 * (tt // 2))
                wo_ps = s_pair[j % 2]
                vector.tensor_copy(pa_sb[:, tt % 2, 0:512], wo_ps[:, 0:512])
                vector.tensor_copy(pa_sb[:, tt % 2, 512:1024], wo_ps[:, 512:1024]).then_inc(sem["s_paev"], 1)

        def p3_tile(t3):
            vector.wait_ge(sem["d_a3"], 32 * (t3 + 1))
            vector.tensor_add(t3_sb[:], a3_sb[:], x3_sb[:]).then_inc(sem["s_t3"], 1)
            x1 = x13_sb[:, t3 % 2, :]
            vector.tensor_add(x1, t3_sb[:], bo_sb[:])
            vector.bn_stats(st3_sb[:, 0:6], x1[:, 0:512])
            vector.bn_stats(st3_sb[:, 6:12], x1[:, 512:1024])
            vector.bn_aggr(ag3_sb[:, 0:2], st3_sb[:, 0:6])
            vector.bn_aggr(ag3_sb[:, 2:4], st3_sb[:, 6:12])
            m1, v1, m2, v2 = (ag3_sb[:, i:i + 1] for i in range(4))
            vector.tensor_add(sc3_sb[:, 0:1], m1, m2)
            vector.tensor_scalar_mul(m3_sb[:, t3 % 2:t3 % 2 + 1], sc3_sb[:, 0:1], 0.5)
            vector.tensor_sub(sc3_sb[:, 2:3], m1, m2)
            vector.tensor_mul(sc3_sb[:, 3:4], sc3_sb[:, 2:3], sc3_sb[:, 2:3])
            vector.tensor_add(sc3_sb[:, 4:5], v1, v2)
            vector.tensor_scalar_mul(sc3_sb[:, 5:6], sc3_sb[:, 4:5], 0.5)
            vector.scalar_tensor_tensor(sc3_sb[:, 6:7], sc3_sb[:, 3:4], 0.25,
                                        sc3_sb[:, 5:6], ALU.mult, ALU.add)
            vector.tensor_scalar(w3_sb[:, t3 % 2:t3 % 2 + 1], sc3_sb[:, 6:7],
                                 1.00001, 1e-10, ALU.mult, ALU.add).then_inc(sem["s_w3"], 1)
            vector.wait_ge(sem["s_r3"], t3 + 1)
            r0 = r3_sb[:, t3 % 2:t3 % 2 + 1]
            vector.tensor_mul(sc3_sb[:, 7:8], r0, r0)
            vector.tensor_mul(sc3_sb[:, 8:9], sc3_sb[:, 7:8], w3_sb[:, t3 % 2:t3 % 2 + 1])
            vector.tensor_scalar(sc3_sb[:, 9:10], sc3_sb[:, 8:9], -0.5, 1.5, ALU.mult, ALU.add)
            vector.tensor_mul(sc3_sb[:, 10:11], r0, sc3_sb[:, 9:10])
            vector.tensor_scalar(z3_sb[:], x1, m3_sb[:, t3 % 2:t3 % 2 + 1],
                                 sc3_sb[:, 10:11], ALU.subtract, ALU.mult)
            vector.tensor_mul(zg3_sb[:], z3_sb[:], g_sb[:])
            vector.tensor_add(t3_sb[:], x1, bbc_sb[:])
            if t3 >= 1:
                vector.wait_ge(sem["d_out"], 16 * t3)
            vector.tensor_add(o3_sb[:], zg3_sb[:], t3_sb[:]).then_inc(sem["s_o3"], 1)

        for j in range(4):
            ln_tile(0, j)
        prep_chain(0)
        for g in range(NG):
            for j in range(4):
                if g < 7:
                    ln_tile(g + 1, j)
                onorm_block(g, j)
            if g < 7:
                prep_chain(g + 1)
            pa_evacs(g)
            if g >= 2:
                p3_tile(2 * (g - 2))
                p3_tile(2 * (g - 2) + 1)
        for t3 in range(12, 16):
            p3_tile(t3)

    @block.scalar
    def _(scalar):
        scalar = _Draining(scalar)
        scalar.wait_ge(sem["d_wc"], 16 * NCONST)

        def r0_tile(gp, j):
            tt = 4 * gp + j
            scalar.wait_ge(sem["s_w"], tt + 1)
            scalar.activation(r0_sb[:, j:j + 1], wln_sb[:, j:j + 1], AF.Ln)
            scalar.activation(r0_sb[:, j:j + 1], r0_sb[:, j:j + 1], AF.Exp, scale=-0.5).then_inc(sem["s_r0"], 1)

        def prep_act(gp):
            for hp in range(4):
                idx = 4 * gp + hp
                scalar.wait_ge(sem["s_qmm"], idx + 1)
                scalar.wait_ge(sem["s_mq"], idx)
                scalar.add(tq_sb[:], p1a_ps[:], bq_sb[:, hp:hp + 1]).then_inc(sem["s_qe"], 1)
                scalar.wait_ge(sem["s_kmm"], idx + 1)
                scalar.wait_ge(sem["s_mk"], idx)
                scalar.add(tk_sb[:], p1b_ps[:], bk_sb[:, hp:hp + 1]).then_inc(sem["s_ke"], 1)
                scalar.wait_ge(sem["s_cq"], idx + 1)
                scalar.activation(qhat[:, gp % 2, hp, :], p1a_ps[:], AF.Copy).then_inc(sem["s_qhe"], 1)
                scalar.wait_ge(sem["s_ck"], idx + 1)
                scalar.activation(khat[:, hp, gp * GT:(gp + 1) * GT], p1b_ps[:], AF.Copy).then_inc(sem["s_khe"], 1)
            scalar.wait_ge(sem["s_b2"], gp + 1)
            scalar.activation(bsc_sb[:, 4:8], bsc_sb[:, 0:4], AF.Ln)
            scalar.activation(bsc_sb[:, 4:8], bsc_sb[:, 4:8], AF.Exp, scale=0.5).then_inc(sem["s_bexp"], 1)

        def p3_act(t3):
            scalar.wait_ge(sem["s_w3"], t3 + 1)
            scalar.activation(r3_sb[:, t3 % 2:t3 % 2 + 1], w3_sb[:, t3 % 2:t3 % 2 + 1], AF.Ln)
            scalar.activation(r3_sb[:, t3 % 2:t3 % 2 + 1], r3_sb[:, t3 % 2:t3 % 2 + 1],
                              AF.Exp, scale=-0.5).then_inc(sem["s_r3"], 1)

        for j in range(4):
            r0_tile(0, j)
        prep_act(0)
        for g in range(NG):
            for i in range(UOFF[g], UOFF[g + 1]):
                gg, hp, kt = UNITS[i]
                scalar.wait_ge(sem["s_qk2"], i + 1)
                if i >= 2:
                    scalar.wait_ge(sem["s_pv"], i - 1)
                if kt == 0 and hp == 0:
                    scalar.wait_ge(sem["s_bias"], g + 1)
                scalar.activation(p_sb[:, i % 2, :], s_pair[i % 2][:], AF.Exp,
                                  bias=bias_sb[:, g % 2, hp:hp + 1]).then_inc(sem["s_exp"], 1)
                if kt == 4 * g + 3:
                    scalar.wait_ge(sem["s_pv"], ULAST[(g, hp)] + 1)
                    scalar.activation(rr_sb[:, 0, :], o_psA[64:96, :], AF.Ln)
                    scalar.activation(rr_sb[:, 0, :], rr_sb[:, 0, :], AF.Exp, scale=-1.0)
                    scalar.activation(rr_sb[:, 1, :], o_psB[64:96, :], AF.Ln)
                    scalar.activation(rr_sb[:, 1, :], rr_sb[:, 1, :], AF.Exp, scale=-1.0).then_inc(sem["s_rr"], 1)
                    if g < 7:
                        r0_tile(g + 1, hp)
            if g < 7:
                prep_act(g + 1)
            if g >= 2:
                p3_act(2 * (g - 2))
                p3_act(2 * (g - 2) + 1)
        for t3 in range(12, 16):
            p3_act(t3)

    @block.gpsimd
    def _(gpsimd):
        gpsimd = _Draining(gpsimd)
        gpsimd.wait_ge(sem["d_wc"], 16 * NCONST)

        for g in range(NG):
            gpsimd.wait_ge(sem["d_pa0"], 16 * 2 * (g + 1))
            gpsimd.wait_ge(sem["d_pa1"], 16 * 2 * (g + 1))
            gpsimd.collective_compute(
                "ReduceScatter", ALU.add,
                replica_groups=[[0, 1], [2, 3], [4, 5], [6, 7]],
                ins=[cc_in.ap()[g * GT:(g + 1) * GT, :].opt()],
                outs=[cc_out.ap()[g * 256:(g + 1) * 256, :].opt()],
            ).then_inc(sem["s_cc"], 1)

    @block.tensor
    def _(tensor):
        tensor.wait_ge(sem["s_cast"], NWC)
        tensor.wait_ge(sem["d_wc"], 16 * NCONST)

        def prep_pe(gp):
            tensor.wait_ge(sem["d_tr0"], 256 * (gp + 1))
            tensor.wait_ge(sem["d_tr1"], 256 * (gp + 1))
            for hp in range(4):
                idx = 4 * gp + hp
                sidx = 2 * gp + hp // 2
                tensor.wait_ge(sem["d_wq%d" % (hp % 2)], 32 * (sidx + 1))
                if idx >= 1:
                    tensor.wait_ge(sem["s_nmax"], 2 * (idx - 1) + 1)
                for f in range(8):
                    ins = tensor.matmul(p1a_ps[:], wqr_sb[:, hp % 2, f, :],
                                        utg_sb[:, f, :], start=(f == 0), stop=(f == 7))
                ins.then_inc(sem["s_qmm"], 1)
                if idx >= 1:
                    tensor.wait_ge(sem["s_nmax"], 2 * (idx - 1) + 2)
                for f in range(8):
                    ins = tensor.matmul(p1b_ps[:], wkr_sb[:, hp % 2, f, :],
                                        utg_sb[:, f, :], start=(f == 0), stop=(f == 7))
                ins.then_inc(sem["s_kmm"], 1)
                tensor.wait_ge(sem["s_qe"], idx + 1)
                tensor.wait_ge(sem["s_mq"], idx + 1)
                tensor.matmul(p1a_ps[:], a1_sb[:], m1q_sb[:], start=True, stop=False)
                tensor.matmul(p1a_ps[:], a2_sb[:], m2q_sb[:], start=False, stop=True).then_inc(sem["s_cq"], 1)
                tensor.wait_ge(sem["s_ke"], idx + 1)
                tensor.wait_ge(sem["s_mk"], idx + 1)
                tensor.matmul(p1b_ps[:], a1_sb[:], m1k_sb[:], start=True, stop=False)
                tensor.matmul(p1b_ps[:], a2_sb[:], m2k_sb[:], start=False, stop=True).then_inc(sem["s_ck"], 1)
                tensor.wait_ge(sem["s_qhe"], idx + 1)
                tensor.wait_ge(sem["s_rq"], idx + 1)
                tensor.matmul(p1a_ps[0:64, :], oblk_sb[:, 0:64], sqq_sb[:],
                              start=True, stop=True).then_inc(sem["s_nmm"], 1)
                tensor.wait_ge(sem["s_khe"], idx + 1)
                tensor.wait_ge(sem["s_rk"], idx + 1)
                tensor.matmul(p1b_ps[0:64, :], oblk_sb[:, 0:64], sqk_sb[:],
                              start=True, stop=True).then_inc(sem["s_nmm"], 1)
            for j in range(4):
                tt = 4 * gp + j
                vp = p1a_ps if j % 2 == 0 else p1b_ps
                if j < 2:
                    tensor.wait_ge(sem["s_nmax"], 8 * gp + 7 + j)
                    if j == 1:
                        tensor.wait_ge(sem["s_bias"], gp + 1)
                else:
                    tensor.wait_ge(sem["s_vev"], tt - 1)
                for f in range(8):
                    ins = tensor.matmul(vp[:], utg_sb[:, f, j * 128:(j + 1) * 128],
                                        wvr_sb[:, f, :], start=(f == 0), stop=(f == 7))
                ins.then_inc(sem["s_vmm"], 1)

        def emit_pv(u):
            g, hp, kt = UNITS[u]
            first, last = (kt == 0), (kt == 4 * g + 3)
            diag = kt >= 4 * g
            q0 = 128 * (kt - 4 * g) if diag else 0
            pa, pb = p_sb[:, u % 2, q0:512], p_sb[:, u % 2, 512 + q0:1024]
            tensor.wait_ge(sem["s_exp"], u + 1)
            if first:
                tensor.wait_ge(sem["s_onorm"], 4 * g + hp + 1)
            tensor.matmul(o_psA[0:64, q0:512], vau[:, hp, kt, 0:64], pa,
                          start=first, stop=last, skip_group_check=True)
            tensor.matmul(o_psA[64:96, q0:512], pvo_sb[:], pa, start=False, stop=last,
                          tile_position=(0, 64), skip_group_check=True)
            tensor.matmul(o_psB[0:64, q0:512], vau[:, hp, kt, 64:128], pb,
                          start=first, stop=last, skip_group_check=True)
            tensor.matmul(o_psB[64:96, q0:512], pvo_sb[:], pb, start=False, stop=last,
                          tile_position=(0, 64), skip_group_check=True).then_inc(sem["s_pv"], 1)

        prep_pe(0)
        for g in range(NG):
            gc2 = (g % 2)
            pend = None
            u = UOFF[g]
            for hp in range(4):
                for kt in range(4 * g + 4):
                    if u >= 2:
                        tensor.wait_ge(sem["s_exp"], u - 1)
                    if hp == 0 and kt == 0:
                        if g > 0:
                            tensor.wait_ge(sem["s_paev"], 4 * g)
                        tensor.wait_ge(sem["s_qhe"], 4 * (g + 1))
                        tensor.wait_ge(sem["s_khe"], 4 * (g + 1))
                        tensor.wait_ge(sem["s_vev"], 4 * (g + 1))
                    kc = slice(kt * 128, (kt + 1) * 128)
                    diag = kt >= 4 * g
                    q0 = 128 * (kt - 4 * g) if diag else 0
                    tensor.matmul(s_pair[u % 2][:, q0:512], khat[0:64, hp, kc],
                                  qhat[0:64, gc2, hp, q0:512], start=True, stop=not diag,
                                  skip_group_check=True)
                    ins2 = tensor.matmul(s_pair[u % 2][:, 512 + q0:1024], khat[64:128, hp, kc],
                                         qhat[64:128, gc2, hp, q0:512], start=True, stop=not diag,
                                         skip_group_check=True)
                    if diag:
                        tensor.matmul(s_pair[u % 2][:, q0:q0 + 128], idn_sb[:],
                                      msk_sb[:], start=False, stop=True, skip_group_check=True)
                        tensor.matmul(s_pair[u % 2][:, 512 + q0:512 + q0 + 128], idn_sb[:],
                                      msk_sb[:], start=False, stop=True,
                                      skip_group_check=True).then_inc(sem["s_qk2"], 1)
                    else:
                        ins2.then_inc(sem["s_qk2"], 1)
                    if pend is not None:
                        emit_pv(pend)
                    pend = u
                    u += 1
            emit_pv(pend)
            if g < 7:
                prep_pe(g + 1)
            for j in range(4):
                tt = 4 * g + j
                if j == 0:
                    tensor.wait_ge(sem["s_onorm"], 4 * (g + 1) + 1)
                if tt >= 2:
                    tensor.wait_ge(sem["s_paev"], tt - 1)
                wo_ps = s_pair[j % 2]
                for half in range(2):
                    for hp in range(4):
                        ins = tensor.matmul(wo_ps[:, half * 512:(half + 1) * 512],
                                            onog[:, hp, j * 128:(j + 1) * 128],
                                            wo_bf[:, hp, half * 512:(half + 1) * 512],
                                            start=(hp == 0), stop=(hp == 3), skip_group_check=True)
                ins.then_inc(sem["s_womm"], 1)

    return nc, ctx


_PROG = None


def _rope_tables():
    inv_freq = (1.0 / 10000.0 ** (np.arange(0, DK, 2, dtype=np.float32) / np.float32(DK))).astype(np.float32)
    pos = np.arange(S, dtype=np.float32)
    fr = (pos[:, None] * inv_freq[None, :]).astype(np.float32)
    c = np.cos(fr.astype(np.float64)).astype(np.float32).T
    s = np.sin(fr.astype(np.float64)).astype(np.float32).T
    t1 = np.ascontiguousarray(np.concatenate([c, c, -s, -s], axis=0))
    t2 = np.ascontiguousarray(np.concatenate([s, s, c, c], axis=0))
    return t1, t2


def kernel(**inputs):
    global _PROG
    if _PROG is None:
        _PROG = _build()
    nc, _ctx = _PROG

    x = np.asarray(inputs["L0_attn_x"], np.float32)
    Wq = np.asarray(inputs["L0_attn_Wq"], np.float32)
    Wk = np.asarray(inputs["L0_attn_Wk"], np.float32)
    Wv = np.asarray(inputs["L0_attn_Wv"], np.float32)
    Wo = np.asarray(inputs["L0_attn_Wo"], np.float32)
    bq = np.asarray(inputs["L0_attn_bq"], np.float32)
    bk = np.asarray(inputs["L0_attn_bk"], np.float32)
    bv = np.asarray(inputs["L0_attn_bv"], np.float32)
    bo = np.asarray(inputs["L0_attn_bo"], np.float32)
    ln_g = np.asarray(inputs["L0_ffn_ln_g"], np.float32)
    ln_b = np.asarray(inputs["L0_ffn_ln_b"], np.float32)

    t1, t2 = _rope_tables()
    kk = np.arange(128)[:, None]
    qq = np.arange(GT)[None, :]
    kk128 = np.arange(128)[:, None]
    qq128 = np.arange(128)[None, :]
    msk = np.where(kk128 <= qq128, np.float32(0), np.float32(-1e9)).astype(BF)
    idn = np.eye(128, dtype=np.float32).astype(BF)
    oblk = np.zeros((128, 64), np.float32)
    oblk[0:64, 0:32] = 1.0
    oblk[64:128, 32:64] = 1.0
    oblk = oblk.astype(BF)
    pvo = np.ones((128, 32), np.float32).astype(BF)
    map1 = np.concatenate([np.arange(32), np.arange(32, 64) + 32,
                           np.arange(64, 96) - 64, np.arange(96, 128) - 32])
    map2 = np.concatenate([np.arange(32) + 32, np.arange(32, 64) + 64,
                           np.arange(64, 96) - 32, np.arange(96, 128)])
    a1 = np.zeros((128, 128), np.float32)
    a1[np.arange(128), map1] = 1.0
    a2 = np.zeros((128, 128), np.float32)
    a2[np.arange(128), map2] = 1.0
    a1 = a1.astype(BF)
    a2 = a2.astype(BF)
    g_bc = np.ascontiguousarray(np.broadcast_to(ln_g, (128, D)), dtype=np.float32)
    b_bc = np.ascontiguousarray(np.broadcast_to(ln_b, (128, D)), dtype=np.float32)
    bo_bc = np.ascontiguousarray(np.broadcast_to(bo, (128, D)), dtype=np.float32)

    in_maps = []
    for c in range(8):
        b, hg = c // 2, c % 2
        heads = [hg * 8 + i for i in range(8)]
        perm = []
        for hp in range(4):
            a, b2 = heads[2 * hp], heads[2 * hp + 1]
            perm += (list(a * 64 + np.arange(0, 64, 2)) + list(b2 * 64 + np.arange(0, 64, 2))
                     + list(a * 64 + np.arange(1, 64, 2)) + list(b2 * 64 + np.arange(1, 64, 2)))
        perm = np.asarray(perm)
        vcols = np.arange(hg * 512, hg * 512 + 512)
        worows = []
        for hp in range(4):
            a, b2 = heads[2 * hp], heads[2 * hp + 1]
            worows += list(a * 64 + np.arange(64)) + list(b2 * 64 + np.arange(64))
        worows = np.asarray(worows)
        own_rows = np.concatenate([512 * gg + 256 * hg + np.arange(256) for gg in range(8)])
        in_maps.append({
            "x_b": np.ascontiguousarray(x[b]),
            "x_own": np.ascontiguousarray(x[b][own_rows]),
            "wq_d": np.ascontiguousarray(Wq[:, perm]).reshape(8, 128, 512),
            "wk_d": np.ascontiguousarray(Wk[:, perm]).reshape(8, 128, 512),
            "wv_d": np.ascontiguousarray(Wv[:, vcols]).reshape(8, 128, 512),
            "wo_d": np.ascontiguousarray(Wo[worows, :]).reshape(4, 128, 1024),
            "bq_d": np.ascontiguousarray(bq[perm].reshape(4, 128).T),
            "bk_d": np.ascontiguousarray(bk[perm].reshape(4, 128).T),
            "bv_d": np.ascontiguousarray(np.broadcast_to(bv[vcols], (128, 512)), dtype=np.float32),
            "t1_d": t1, "t2_d": t2, "msk_d": np.ascontiguousarray(msk),
            "idn_d": idn, "oblk_d": oblk, "pvo_d": pvo,
            "a1_d": a1, "a2_d": a2,
            "g_d": g_bc, "bb_d": b_bc, "bo_d": bo_bc,
        })

    res = run_bass_kernel_spmd(nc, in_maps, core_ids=list(range(8)))
    out = np.empty((B, S, D), np.float32)
    for c in range(8):
        b, hg = c // 2, c % 2
        o = np.asarray(res.results[c]["out"]).reshape(8, 256, D)
        for gg in range(8):
            out[b, 512 * gg + 256 * hg: 512 * gg + 256 * (hg + 1)] = o[gg]
    return out


# version 12
# speedup vs baseline: 1.1017x; 1.1017x over previous
import numpy as np
import ml_dtypes
from contextlib import ExitStack

import concourse.bass as bass
import concourse.mybir as mybir
from concourse.bass_utils import run_bass_kernel_spmd

F32 = mybir.dt.float32
BF16 = mybir.dt.bfloat16
AF = mybir.ActivationFunctionType
ALU = mybir.AluOpType
AXX = mybir.AxisListType.X
BF = ml_dtypes.bfloat16

B, S, D, H, DK = 4, 4096, 1024, 16, 64
NG, GT = 8, 512
NT = S // 128
NHP = 4

UNITS = [(g, hp, kt) for g in range(NG) for hp in range(NHP) for kt in range(4 * g + 4)]
NU = len(UNITS)
ULAST = {}
for i, (g, hp, kt) in enumerate(UNITS):
    ULAST[(g, hp)] = i
UOFF = [0] * (NG + 1)
for g in range(NG):
    UOFF[g + 1] = UOFF[g] + 4 * (4 * g + 4)

NWC = 28
NCONST = 11

_DRAIN_OPS = {
    "tensor_copy", "tensor_tensor", "tensor_add", "tensor_sub", "tensor_mul",
    "tensor_max", "tensor_scalar", "tensor_single_scalar", "tensor_scalar_add",
    "tensor_scalar_sub", "tensor_scalar_mul", "tensor_scalar_max", "tensor_scalar_min",
    "tensor_reduce", "reduce_max", "reduce_sum", "scalar_tensor_tensor",
    "bn_stats", "bn_aggr", "reciprocal", "memset", "memzero", "activation",
    "add", "mul", "copy", "square", "sqrt", "select", "transpose",
}
_DRAINS_ON = [True]


def _ap_tensor_names(args):
    names = []
    for a in args:
        t = getattr(a, "tensor", None)
        if t is not None:
            names.append(getattr(t, "name", str(t)))
    return names


class _Draining:

    def __init__(self, eng):
        self._e = eng
        self._written = set()
        self._read = set()

    def __getattr__(self, name):
        attr = getattr(self._e, name)
        if name in _DRAIN_OPS and _DRAINS_ON[0]:
            def wrapped(*a, **k):
                names = _ap_tensor_names(list(a) + list(k.values()))
                out = names[0] if names else None
                need = any(n in self._written for n in names) or (out in self._read)
                if need:
                    self._e.drain()
                    self._written.clear()
                    self._read.clear()
                if out is not None:
                    self._written.add(out)
                for n in names[1:]:
                    self._read.add(n)
                return attr(*a, **k)
            return wrapped
        return attr


def _build(drains=True):
    _DRAINS_ON[0] = drains
    nc = bass.Bass("TRN2", target_bir_lowering=False, debug=False)

    def din(name, shape, dt):
        return nc.dram_tensor(name, shape, dt, kind="ExternalInput").ap()

    x_b = din("x_b", [S, D], F32)
    x_own = din("x_own", [S // 2, D], F32)
    wq_d = din("wq_d", [8, 128, 512], F32)
    wk_d = din("wk_d", [8, 128, 512], F32)
    wv_d = din("wv_d", [8, 128, 512], F32)
    wo_d = din("wo_d", [4, 128, 1024], F32)
    bq_d = din("bq_d", [128, 4], F32)
    bk_d = din("bk_d", [128, 4], F32)
    bv_d = din("bv_d", [128, 512], F32)
    t1_d = din("t1_d", [128, S], F32)
    t2_d = din("t2_d", [128, S], F32)
    msk_d = din("msk_d", [128, 128], BF16)
    idn_d = din("idn_d", [128, 128], BF16)
    oblk_d = din("oblk_d", [128, 64], BF16)
    g_d = din("g_d", [128, D], F32)
    bb_d = din("bb_d", [128, D], F32)
    bo_d = din("bo_d", [128, D], F32)
    a1_d = din("a1_d", [128, 128], BF16)
    a2_d = din("a2_d", [128, 128], BF16)
    out_d = nc.dram_tensor("out", [S // 2, D], F32, kind="ExternalOutput").ap()
    cc_in = nc.dram_tensor("cc_in", [S, D], F32)
    cc_out = nc.dram_tensor("cc_out", [S // 2, D], F32)
    wqh_d = nc.dram_tensor("wqh_d", [4, 8, 128, 128], BF16)
    wkh_d = nc.dram_tensor("wkh_d", [4, 8, 128, 128], BF16)

    ctx = ExitStack()
    sem = {}
    for name in ["d_ws0", "d_wc", "d_wb", "d_x0", "d_x1", "d_t0", "d_t1",
                 "d_tr0", "d_tr1", "d_wq0", "d_wq1",
                 "d_pa0", "d_pa1", "d_a3", "d_out",
                 "s_cast", "s_w", "s_r0", "s_ln",
                 "s_qmm", "s_kmm", "s_qe", "s_ke",
                 "s_mq", "s_mk", "s_cq", "s_ck", "s_qhe", "s_khe",
                 "s_rq", "s_rk", "s_nmm", "s_nmax", "s_vmm", "s_vev",
                 "s_b2", "s_bexp", "s_bias",
                 "s_exp", "s_qk2", "s_pv", "s_rr", "s_onorm",
                 "s_womm", "s_paev", "s_cc",
                 "s_t3", "s_x13", "s_w3", "s_r3", "s_z3", "s_zg3", "s_o3"]:
        sem[name] = ctx.enter_context(nc.semaphore(name))

    gsb = lambda n, sh, dt: ctx.enter_context(nc.sbuf_tensor(n, sh, dt))
    idn_sb = gsb("idn_sb", [128, 128], BF16)
    oblk_sb = gsb("oblk_sb", [128, 64], BF16)
    a1_sb = gsb("a1_sb", [128, 128], BF16)
    a2_sb = gsb("a2_sb", [128, 128], BF16)
    bias_sb = gsb("bias_sb", [128, 2, 4], F32)
    nmaxq_sb = gsb("nmaxq_sb", [64, 4], F32)
    nmaxk_sb = gsb("nmaxk_sb", [64, 4], F32)
    nk2_sb = gsb("nk2_sb", [64, 4], F32)
    bsc_sb = gsb("bsc_sb", [64, 12], F32)
    bq_sb = gsb("bq_sb", [128, 4], F32)
    bk_sb = gsb("bk_sb", [128, 4], F32)
    bv_sb = gsb("bv_sb", [128, 512], F32)
    khat = gsb("khat", [128, 4, S], BF16)
    qhat = gsb("qhat", [128, 2, 4, GT], BF16)
    vau = gsb("vau", [128, 4, NT, 192], BF16)
    onog = gsb("onog", [128, 4, GT], BF16)
    wo_bf = gsb("wo_bf", [128, 4, 1024], BF16)
    pa_sb = gsb("pa_sb", [128, 2, 1024], F32)
    stg_sb = pa_sb[:, 0, :]
    p_sb = gsb("p_sb", [128, 2, 1024], BF16)
    msk_sb = gsb("msk_sb", [128, 128], BF16)
    rr_sb = gsb("rr_sb", [32, 2, GT], F32)
    wcast_sb = gsb("wcast_sb", [128, 512], BF16)
    wvr_sb = gsb("wvr_sb", [128, 8, 512], BF16)
    wqr_sb = gsb("wqr_sb", [128, 2, 8, 128], BF16)
    wkr_sb = gsb("wkr_sb", [128, 2, 8, 128], BF16)
    t1_sb = gsb("t1_sb", [128, 2, GT], F32)
    t2_sb = gsb("t2_sb", [128, 2, GT], F32)
    x_sb = gsb("x_sb", [128, 2, 1024], F32)
    ubf_sb = gsb("ubf_sb", [128, 2, 1024], BF16)
    utg_sb = gsb("utg_sb", [128, 8, GT], BF16)
    tq_sb = gsb("tq_sb", [128, GT], F32)
    tk_sb = gsb("tk_sb", [128, GT], F32)
    m1q_sb = gsb("m1q_sb", [128, GT], BF16)
    m2q_sb = gsb("m2q_sb", [128, GT], BF16)
    m1k_sb = gsb("m1k_sb", [128, GT], BF16)
    m2k_sb = gsb("m2k_sb", [128, GT], BF16)
    sqq_sb = gsb("sqq_sb", [128, GT], BF16)
    sqk_sb = gsb("sqk_sb", [128, GT], BF16)
    st_sb = gsb("st_sb", [128, 12], F32)
    ag_sb = gsb("ag_sb", [128, 4], F32)
    sc_sb = gsb("sc_sb", [128, 16], F32)
    wln_sb = gsb("wln_sb", [128, 4], F32)
    r0_sb = gsb("r0_sb", [128, 4], F32)
    g_sb = gsb("g_sb", [128, D], F32)
    bbc_sb = gsb("bbc_sb", [128, D], F32)
    bo_sb = gsb("bo_sb", [128, D], F32)
    x3_sb = gsb("x3_sb", [128, 1024], F32)
    t3_sb = gsb("t3_sb", [128, 1024], F32)
    x13_sb = gsb("x13_sb", [128, 2, 1024], F32)
    zg3_sb = gsb("zg3_sb", [128, 1024], F32)
    st3_sb = gsb("st3_sb", [128, 12], F32)
    ag3_sb = gsb("ag3_sb", [128, 4], F32)
    sc3_sb = gsb("sc3_sb", [128, 16], F32)
    m3_sb = gsb("m3_sb", [128, 2], F32)
    w3_sb = gsb("w3_sb", [128, 2], F32)
    r3_sb = gsb("r3_sb", [128, 2], F32)

    s_pair = [ctx.enter_context(nc.psum_tensor(f"s_pair{i}", [128, 1024], F32)) for i in range(2)]
    o_psA = ctx.enter_context(nc.psum_tensor("o_psA", [128, 512], F32))
    o_psB = ctx.enter_context(nc.psum_tensor("o_psB", [128, 512], F32))
    p1a_ps = ctx.enter_context(nc.psum_tensor("p1a_ps", [128, 512], F32))
    p1b_ps = ctx.enter_context(nc.psum_tensor("p1b_ps", [128, 512], F32))

    block = ctx.enter_context(nc.Block())
    CHUNKS = ([("wq", f) for f in range(8)] + [("wk", f) for f in range(8)]
              + [("wv", f) for f in range(8)] + [("wo", f) for f in range(4)])

    @block.sync
    def _(sync):
        nwb = 0
        for i, (w, f) in enumerate(CHUNKS):
            if i >= 1:
                sync.wait_ge(sem["s_cast"], i)
            if i >= 1 and CHUNKS[i - 1][0] in ("wq", "wk"):
                pw, pf = CHUNKS[i - 1]
                dstd = wqh_d if pw == "wq" else wkh_d
                for hp in range(4):
                    sync.dma_start(out=dstd.ap()[hp, pf, :, :],
                                   in_=wcast_sb[:, hp * 128:(hp + 1) * 128]).then_inc(sem["d_wb"], 16)
                    nwb += 1
            src = {"wq": wq_d, "wk": wk_d, "wv": wv_d, "wo": wo_d}[w][f]
            width = 1024 if w == "wo" else 512
            sync.dma_start(out=stg_sb[:, 0:width], in_=src).then_inc(sem["d_ws0"], 16)
        for src, dst in [(bq_d, bq_sb[:]), (bk_d, bk_sb[:]), (bv_d, bv_sb[:]),
                         (idn_d, idn_sb[:]), (oblk_d, oblk_sb[:]),
                         (msk_d, msk_sb[:]),
                         (a1_d, a1_sb[:]), (a2_d, a2_sb[:]),
                         (g_d, g_sb[:]), (bb_d, bbc_sb[:]), (bo_d, bo_sb[:])]:
            sync.dma_start(out=dst, in_=src).then_inc(sem["d_wc"], 16)
        sync.wait_ge(sem["s_cast"], 16)
        sync.wait_ge(sem["d_wb"], 16 * 64)

        def prep_dmas(gp):
            for j in range(4):
                tt = 4 * gp + j
                if tt >= 2:
                    sync.wait_ge(sem["s_ln"], tt - 1)
                sync.dma_start(out=x_sb[:, tt % 2, :],
                               in_=x_b[tt * 128:(tt + 1) * 128, :]).then_inc(sem["d_x%d" % (tt % 2)], 16)
            if gp >= 2:
                sync.wait_ge(sem["s_mq"], 4 * (gp - 1))
                sync.wait_ge(sem["s_mk"], 4 * (gp - 1))
            sync.dma_start(out=t1_sb[:, gp % 2, :], in_=t1_d[:, gp * GT:(gp + 1) * GT]).then_inc(sem["d_t%d" % (gp % 2)], 16)
            sync.dma_start(out=t2_sb[:, gp % 2, :], in_=t2_d[:, gp * GT:(gp + 1) * GT]).then_inc(sem["d_t%d" % (gp % 2)], 16)
            for j in range(4):
                tt = 4 * gp + j
                sync.wait_ge(sem["s_ln"], tt + 1)
                if gp >= 1:
                    sync.wait_ge(sem["s_vmm"], 4 * gp)
                for cch in range(8):
                    sync.dma_start_transpose(utg_sb[:, cch, j * 128:(j + 1) * 128],
                                             ubf_sb[:, tt % 2, cch * 128:(cch + 1) * 128]).then_inc(sem["d_tr%d" % (tt % 2)], 16)
            for hp in range(4):
                idx = 4 * gp + hp
                if idx >= 2:
                    sync.wait_ge(sem["s_qmm"], idx - 1)
                    sync.wait_ge(sem["s_kmm"], idx - 1)
                sync.dma_start(out=wqr_sb[:, hp % 2, :, :],
                               in_=wqh_d.ap()[hp].rearrange("f p c -> p f c")).then_inc(sem["d_wq%d" % (hp % 2)], 16)
                sync.dma_start(out=wkr_sb[:, hp % 2, :, :],
                               in_=wkh_d.ap()[hp].rearrange("f p c -> p f c")).then_inc(sem["d_wq%d" % (hp % 2)], 16)

        prep_dmas(0)
        for g in range(NG):
            if g < 7:
                prep_dmas(g + 1)
            for j in range(4):
                tt = 4 * g + j
                sync.wait_ge(sem["s_paev"], tt + 1)
                sync.dma_start(out=cc_in.ap()[tt * 128:(tt + 1) * 128, :],
                               in_=pa_sb[:, tt % 2, :]).then_inc(sem["d_pa%d" % (tt % 2)], 16)
            if g >= 2:
                for t3 in (2 * (g - 2), 2 * (g - 2) + 1):
                    sync.wait_ge(sem["s_cc"], t3 // 2 + 1)
                    sync.wait_ge(sem["s_o3"], t3)
                    sync.dma_start(out=t3_sb[:],
                                   in_=cc_out.ap()[t3 * 128:(t3 + 1) * 128, :]).then_inc(sem["d_a3"], 16)
                    sync.dma_start(out=x3_sb[:],
                                   in_=x_own[t3 * 128:(t3 + 1) * 128, :]).then_inc(sem["d_a3"], 16)
                for t3 in (2 * (g - 2), 2 * (g - 2) + 1):
                    sync.wait_ge(sem["s_o3"], t3 + 1)
                    sync.dma_start(out=out_d[t3 * 128:(t3 + 1) * 128, :],
                                   in_=zg3_sb[:]).then_inc(sem["d_out"], 16)
        for gg in (6, 7):
            for t3 in (2 * gg, 2 * gg + 1):
                sync.wait_ge(sem["s_cc"], t3 // 2 + 1)
                sync.wait_ge(sem["s_o3"], t3)
                sync.dma_start(out=t3_sb[:],
                               in_=cc_out.ap()[t3 * 128:(t3 + 1) * 128, :]).then_inc(sem["d_a3"], 16)
                sync.dma_start(out=x3_sb[:],
                               in_=x_own[t3 * 128:(t3 + 1) * 128, :]).then_inc(sem["d_a3"], 16)
            for t3 in (2 * gg, 2 * gg + 1):
                sync.wait_ge(sem["s_o3"], t3 + 1)
                sync.dma_start(out=out_d[t3 * 128:(t3 + 1) * 128, :],
                               in_=zg3_sb[:]).then_inc(sem["d_out"], 16)
        sync.wait_ge(sem["d_out"], 16 * 16)

    @block.vector
    def _(vector):
        vector = _Draining(vector)
        nwbk = 0
        for i, (w, f) in enumerate(CHUNKS):
            vector.wait_ge(sem["d_ws0"], 16 * (i + 1))
            if w in ("wq", "wk"):
                if nwbk >= 1:
                    vector.wait_ge(sem["d_wb"], 64 * nwbk)
                vector.tensor_copy(wcast_sb[:], stg_sb[:, 0:512]).then_inc(sem["s_cast"], 1)
                nwbk += 1
            elif w == "wv":
                vector.tensor_copy(wvr_sb[:, f, :], stg_sb[:, 0:512]).then_inc(sem["s_cast"], 1)
            else:
                vector.tensor_copy(wo_bf[:, f, :], stg_sb[:, 0:1024]).then_inc(sem["s_cast"], 1)
        vector.wait_ge(sem["d_wc"], 16 * NCONST)
        vector.memset(s_pair[0][:], 0.0)
        vector.memset(s_pair[1][:], 0.0).then_inc(sem["s_onorm"], 1)
        vector.memset(vau[:, :, :, 64:96], 1.0)
        vector.memset(vau[:, :, :, 160:192], 1.0)

        def ln_tile(gp, j):
            tt = 4 * gp + j
            vector.wait_ge(sem["d_x%d" % (tt % 2)], 16 * (tt // 2 + 1))
            if tt >= 2:
                vector.wait_ge(sem["d_tr%d" % (tt % 2)], 128 * (tt // 2))
            xt = x_sb[:, tt % 2, :]
            vector.bn_stats(st_sb[:, 0:6], xt[:, 0:512])
            vector.bn_stats(st_sb[:, 6:12], xt[:, 512:1024])
            vector.bn_aggr(ag_sb[:, 0:2], st_sb[:, 0:6])
            vector.bn_aggr(ag_sb[:, 2:4], st_sb[:, 6:12])
            m1, v1, m2, v2 = (ag_sb[:, i:i + 1] for i in range(4))
            vector.tensor_add(sc_sb[:, 0:1], m1, m2)
            vector.tensor_scalar_mul(sc_sb[:, 1:2], sc_sb[:, 0:1], 0.5)
            vector.tensor_sub(sc_sb[:, 2:3], m1, m2)
            vector.tensor_mul(sc_sb[:, 3:4], sc_sb[:, 2:3], sc_sb[:, 2:3])
            vector.tensor_add(sc_sb[:, 4:5], v1, v2)
            vector.tensor_scalar_mul(sc_sb[:, 5:6], sc_sb[:, 4:5], 0.5)
            vector.scalar_tensor_tensor(sc_sb[:, 6:7], sc_sb[:, 3:4], 0.25,
                                        sc_sb[:, 5:6], ALU.mult, ALU.add)
            vector.tensor_scalar_add(wln_sb[:, j:j + 1], sc_sb[:, 6:7], 1e-5).then_inc(sem["s_w"], 1)
            vector.wait_ge(sem["s_r0"], tt + 1)
            r0 = r0_sb[:, j:j + 1]
            vector.tensor_mul(sc_sb[:, 7:8], r0, r0)
            vector.tensor_mul(sc_sb[:, 8:9], sc_sb[:, 7:8], wln_sb[:, j:j + 1])
            vector.tensor_scalar(sc_sb[:, 9:10], sc_sb[:, 8:9], -0.5, 1.5, ALU.mult, ALU.add)
            vector.tensor_mul(sc_sb[:, 10:11], r0, sc_sb[:, 9:10])
            vector.tensor_scalar(ubf_sb[:, tt % 2, :], xt, sc_sb[:, 1:2],
                                 sc_sb[:, 10:11], ALU.subtract, ALU.mult).then_inc(sem["s_ln"], 1)

        def onorm_block(g, hp):
            vector.wait_ge(sem["s_rr"], 4 * g + hp + 1)
            vector.tensor_mul(onog[0:32, hp, :], o_psA[0:32, :], rr_sb[:, 0, :])
            vector.tensor_mul(onog[32:64, hp, :], o_psA[32:64, :], rr_sb[:, 0, :])
            vector.tensor_mul(onog[64:96, hp, :], o_psB[0:32, :], rr_sb[:, 1, :])
            vector.tensor_mul(onog[96:128, hp, :], o_psB[32:64, :], rr_sb[:, 1, :]).then_inc(sem["s_onorm"], 1)

        def prep_chain(gp):
            for hp in range(4):
                idx = 4 * gp + hp
                vector.wait_ge(sem["s_qe"], idx + 1)
                vector.wait_ge(sem["s_cq"], idx)
                vector.wait_ge(sem["d_t%d" % (gp % 2)], 32 * (gp // 2 + 1))
                t1g, t2g = t1_sb[:, gp % 2, :], t2_sb[:, gp % 2, :]
                vector.scalar_tensor_tensor(m1q_sb[:], tq_sb[:], 0.125, t1g, ALU.mult, ALU.mult)
                vector.scalar_tensor_tensor(m2q_sb[:], tq_sb[:], 0.125, t2g, ALU.mult, ALU.mult).then_inc(sem["s_mq"], 1)
                vector.wait_ge(sem["s_ke"], idx + 1)
                vector.wait_ge(sem["s_ck"], idx)
                vector.tensor_mul(m1k_sb[:], tk_sb[:], t1g)
                vector.tensor_mul(m2k_sb[:], tk_sb[:], t2g).then_inc(sem["s_mk"], 1)
                vector.wait_ge(sem["s_qhe"], idx + 1)
                vector.tensor_mul(sqq_sb[:], qhat[:, gp % 2, hp, :], qhat[:, gp % 2, hp, :]).then_inc(sem["s_rq"], 1)
                vector.wait_ge(sem["s_khe"], idx + 1)
                if idx >= 1:
                    vector.wait_ge(sem["s_nmm"], 2 * idx)
                vector.tensor_mul(sqk_sb[:], khat[:, hp, gp * GT:(gp + 1) * GT],
                                  khat[:, hp, gp * GT:(gp + 1) * GT]).then_inc(sem["s_rk"], 1)
                vector.wait_ge(sem["s_nmm"], 2 * idx + 1)
                vector.reduce_max(nmaxq_sb[:, hp:hp + 1], p1a_ps[0:64, :], AXX).then_inc(sem["s_nmax"], 1)
                vector.wait_ge(sem["s_nmm"], 2 * idx + 2)
                vector.reduce_max(nmaxk_sb[:, hp:hp + 1], p1b_ps[0:64, :], AXX).then_inc(sem["s_nmax"], 1)
            if gp == 0:
                vector.tensor_copy(nk2_sb[:], nmaxk_sb[:])
            else:
                vector.tensor_max(nk2_sb[:], nk2_sb[:], nmaxk_sb[:])
            vector.tensor_mul(bsc_sb[:, 0:4], nmaxq_sb[:], nk2_sb[:]).then_inc(sem["s_b2"], 1)
            vector.wait_ge(sem["s_bexp"], gp + 1)
            vector.tensor_scalar_mul(bsc_sb[:, 8:12], bsc_sb[:, 4:8], -1.0)
            vector.tensor_copy(p1b_ps[32:64, 0:4], bsc_sb[32:64, 8:12])
            vector.tensor_tensor(bias_sb[0:32, gp % 2, :], bsc_sb[0:32, 8:12],
                                 p1b_ps[32:64, 0:4], ALU.min)
            vector.tensor_copy(bias_sb[32:64, gp % 2, :], bias_sb[0:32, gp % 2, :])
            vector.tensor_copy(bias_sb[64:128, gp % 2, :], bias_sb[0:64, gp % 2, :]).then_inc(sem["s_bias"], 1)
            for j in range(4):
                tt = 4 * gp + j
                vector.wait_ge(sem["s_vmm"], tt + 1)
                vp = p1a_ps if j % 2 == 0 else p1b_ps
                vector.tensor_add(
                    vau[:, :, tt, :].rearrange("p h (t c) -> p h t c", t=2)[:, :, :, 0:64],
                    vp[:].rearrange("p (h t c) -> p h t c", h=4, t=2),
                    bv_sb[:].rearrange("p (h t c) -> p h t c", h=4, t=2),
                ).then_inc(sem["s_vev"], 1)

        def pa_evacs(g):
            for j in range(4):
                tt = 4 * g + j
                vector.wait_ge(sem["s_womm"], tt + 1)
                if tt >= 2:
                    vector.wait_ge(sem["d_pa%d" % (tt % 2)], 16 * (tt // 2))
                wo_ps = s_pair[j % 2]
                vector.tensor_copy(pa_sb[:, tt % 2, 0:512], wo_ps[:, 0:512])
                vector.tensor_copy(pa_sb[:, tt % 2, 512:1024], wo_ps[:, 512:1024]).then_inc(sem["s_paev"], 1)

        def p3_tile(t3):
            vector.wait_ge(sem["d_a3"], 32 * (t3 + 1))
            vector.tensor_add(t3_sb[:], t3_sb[:], x3_sb[:]).then_inc(sem["s_t3"], 1)
            x1 = x13_sb[:, t3 % 2, :]
            vector.tensor_add(x1, t3_sb[:], bo_sb[:])
            vector.bn_stats(st3_sb[:, 0:6], x1[:, 0:512])
            vector.bn_stats(st3_sb[:, 6:12], x1[:, 512:1024])
            vector.bn_aggr(ag3_sb[:, 0:2], st3_sb[:, 0:6])
            vector.bn_aggr(ag3_sb[:, 2:4], st3_sb[:, 6:12])
            m1, v1, m2, v2 = (ag3_sb[:, i:i + 1] for i in range(4))
            vector.tensor_add(sc3_sb[:, 0:1], m1, m2)
            vector.tensor_scalar_mul(m3_sb[:, t3 % 2:t3 % 2 + 1], sc3_sb[:, 0:1], 0.5)
            vector.tensor_sub(sc3_sb[:, 2:3], m1, m2)
            vector.tensor_mul(sc3_sb[:, 3:4], sc3_sb[:, 2:3], sc3_sb[:, 2:3])
            vector.tensor_add(sc3_sb[:, 4:5], v1, v2)
            vector.tensor_scalar_mul(sc3_sb[:, 5:6], sc3_sb[:, 4:5], 0.5)
            vector.scalar_tensor_tensor(sc3_sb[:, 6:7], sc3_sb[:, 3:4], 0.25,
                                        sc3_sb[:, 5:6], ALU.mult, ALU.add)
            vector.tensor_scalar(w3_sb[:, t3 % 2:t3 % 2 + 1], sc3_sb[:, 6:7],
                                 1.00001, 1e-10, ALU.mult, ALU.add).then_inc(sem["s_w3"], 1)
            vector.wait_ge(sem["s_r3"], t3 + 1)
            r0 = r3_sb[:, t3 % 2:t3 % 2 + 1]
            vector.tensor_mul(sc3_sb[:, 7:8], r0, r0)
            vector.tensor_mul(sc3_sb[:, 8:9], sc3_sb[:, 7:8], w3_sb[:, t3 % 2:t3 % 2 + 1])
            vector.tensor_scalar(sc3_sb[:, 9:10], sc3_sb[:, 8:9], -0.5, 1.5, ALU.mult, ALU.add)
            vector.tensor_mul(sc3_sb[:, 10:11], r0, sc3_sb[:, 9:10])
            if t3 >= 1:
                vector.wait_ge(sem["d_out"], 16 * t3)
            vector.tensor_scalar(zg3_sb[:], x1, m3_sb[:, t3 % 2:t3 % 2 + 1],
                                 sc3_sb[:, 10:11], ALU.subtract, ALU.mult)
            vector.tensor_mul(zg3_sb[:], zg3_sb[:], g_sb[:])
            vector.tensor_add(t3_sb[:], x1, bbc_sb[:])
            vector.tensor_add(zg3_sb[:], zg3_sb[:], t3_sb[:]).then_inc(sem["s_o3"], 1)

        for j in range(4):
            ln_tile(0, j)
        prep_chain(0)
        for g in range(NG):
            for j in range(4):
                if g < 7:
                    ln_tile(g + 1, j)
                onorm_block(g, j)
            if g < 7:
                prep_chain(g + 1)
            pa_evacs(g)
            if g >= 2:
                p3_tile(2 * (g - 2))
                p3_tile(2 * (g - 2) + 1)
        for t3 in range(12, 16):
            p3_tile(t3)

    @block.scalar
    def _(scalar):
        scalar = _Draining(scalar)
        scalar.wait_ge(sem["d_wc"], 16 * NCONST)

        def r0_tile(gp, j):
            tt = 4 * gp + j
            scalar.wait_ge(sem["s_w"], tt + 1)
            scalar.activation(r0_sb[:, j:j + 1], wln_sb[:, j:j + 1], AF.Ln)
            scalar.activation(r0_sb[:, j:j + 1], r0_sb[:, j:j + 1], AF.Exp, scale=-0.5).then_inc(sem["s_r0"], 1)

        def prep_act(gp):
            for hp in range(4):
                idx = 4 * gp + hp
                scalar.wait_ge(sem["s_qmm"], idx + 1)
                scalar.wait_ge(sem["s_mq"], idx)
                scalar.add(tq_sb[:], p1a_ps[:], bq_sb[:, hp:hp + 1]).then_inc(sem["s_qe"], 1)
                scalar.wait_ge(sem["s_kmm"], idx + 1)
                scalar.wait_ge(sem["s_mk"], idx)
                scalar.add(tk_sb[:], p1b_ps[:], bk_sb[:, hp:hp + 1]).then_inc(sem["s_ke"], 1)
                scalar.wait_ge(sem["s_cq"], idx + 1)
                scalar.activation(qhat[:, gp % 2, hp, :], p1a_ps[:], AF.Copy).then_inc(sem["s_qhe"], 1)
                scalar.wait_ge(sem["s_ck"], idx + 1)
                scalar.activation(khat[:, hp, gp * GT:(gp + 1) * GT], p1b_ps[:], AF.Copy).then_inc(sem["s_khe"], 1)
            scalar.wait_ge(sem["s_b2"], gp + 1)
            scalar.activation(bsc_sb[:, 4:8], bsc_sb[:, 0:4], AF.Ln)
            scalar.activation(bsc_sb[:, 4:8], bsc_sb[:, 4:8], AF.Exp, scale=0.5).then_inc(sem["s_bexp"], 1)

        def p3_act(t3):
            scalar.wait_ge(sem["s_w3"], t3 + 1)
            scalar.activation(r3_sb[:, t3 % 2:t3 % 2 + 1], w3_sb[:, t3 % 2:t3 % 2 + 1], AF.Ln)
            scalar.activation(r3_sb[:, t3 % 2:t3 % 2 + 1], r3_sb[:, t3 % 2:t3 % 2 + 1],
                              AF.Exp, scale=-0.5).then_inc(sem["s_r3"], 1)

        for j in range(4):
            r0_tile(0, j)
        prep_act(0)
        for g in range(NG):
            for i in range(UOFF[g], UOFF[g + 1]):
                gg, hp, kt = UNITS[i]
                scalar.wait_ge(sem["s_qk2"], i + 1)
                if i >= 2:
                    scalar.wait_ge(sem["s_pv"], i - 1)
                if kt == 0 and hp == 0:
                    scalar.wait_ge(sem["s_bias"], g + 1)
                scalar.activation(p_sb[:, i % 2, :], s_pair[i % 2][:], AF.Exp,
                                  bias=bias_sb[:, g % 2, hp:hp + 1]).then_inc(sem["s_exp"], 1)
                if kt == 4 * g + 3:
                    scalar.wait_ge(sem["s_pv"], ULAST[(g, hp)] + 1)
                    scalar.activation(rr_sb[:, 0, :], o_psA[64:96, :], AF.Ln)
                    scalar.activation(rr_sb[:, 0, :], rr_sb[:, 0, :], AF.Exp, scale=-1.0)
                    scalar.activation(rr_sb[:, 1, :], o_psB[64:96, :], AF.Ln)
                    scalar.activation(rr_sb[:, 1, :], rr_sb[:, 1, :], AF.Exp, scale=-1.0).then_inc(sem["s_rr"], 1)
                    if g < 7:
                        r0_tile(g + 1, hp)
            if g < 7:
                prep_act(g + 1)
            if g >= 2:
                p3_act(2 * (g - 2))
                p3_act(2 * (g - 2) + 1)
        for t3 in range(12, 16):
            p3_act(t3)

    @block.gpsimd
    def _(gpsimd):
        gpsimd = _Draining(gpsimd)
        gpsimd.wait_ge(sem["d_wc"], 16 * NCONST)

        for g in range(NG):
            gpsimd.wait_ge(sem["d_pa0"], 16 * 2 * (g + 1))
            gpsimd.wait_ge(sem["d_pa1"], 16 * 2 * (g + 1))
            gpsimd.collective_compute(
                "ReduceScatter", ALU.add,
                replica_groups=[[0, 1], [2, 3], [4, 5], [6, 7]],
                ins=[cc_in.ap()[g * GT:(g + 1) * GT, :].opt()],
                outs=[cc_out.ap()[g * 256:(g + 1) * 256, :].opt()],
            ).then_inc(sem["s_cc"], 1)

    @block.tensor
    def _(tensor):
        tensor.wait_ge(sem["s_cast"], NWC)
        tensor.wait_ge(sem["d_wc"], 16 * NCONST)

        def prep_pe(gp):
            tensor.wait_ge(sem["d_tr0"], 256 * (gp + 1))
            tensor.wait_ge(sem["d_tr1"], 256 * (gp + 1))
            for hp in range(4):
                idx = 4 * gp + hp
                sidx = 2 * gp + hp // 2
                tensor.wait_ge(sem["d_wq%d" % (hp % 2)], 32 * (sidx + 1))
                if idx >= 1:
                    tensor.wait_ge(sem["s_nmax"], 2 * (idx - 1) + 1)
                for f in range(8):
                    ins = tensor.matmul(p1a_ps[:], wqr_sb[:, hp % 2, f, :],
                                        utg_sb[:, f, :], start=(f == 0), stop=(f == 7))
                ins.then_inc(sem["s_qmm"], 1)
                if idx >= 1:
                    tensor.wait_ge(sem["s_nmax"], 2 * (idx - 1) + 2)
                for f in range(8):
                    ins = tensor.matmul(p1b_ps[:], wkr_sb[:, hp % 2, f, :],
                                        utg_sb[:, f, :], start=(f == 0), stop=(f == 7))
                ins.then_inc(sem["s_kmm"], 1)
                tensor.wait_ge(sem["s_qe"], idx + 1)
                tensor.wait_ge(sem["s_mq"], idx + 1)
                tensor.matmul(p1a_ps[:], a1_sb[:], m1q_sb[:], start=True, stop=False)
                tensor.matmul(p1a_ps[:], a2_sb[:], m2q_sb[:], start=False, stop=True).then_inc(sem["s_cq"], 1)
                tensor.wait_ge(sem["s_ke"], idx + 1)
                tensor.wait_ge(sem["s_mk"], idx + 1)
                tensor.matmul(p1b_ps[:], a1_sb[:], m1k_sb[:], start=True, stop=False)
                tensor.matmul(p1b_ps[:], a2_sb[:], m2k_sb[:], start=False, stop=True).then_inc(sem["s_ck"], 1)
                tensor.wait_ge(sem["s_qhe"], idx + 1)
                tensor.wait_ge(sem["s_rq"], idx + 1)
                tensor.matmul(p1a_ps[0:64, :], oblk_sb[:, 0:64], sqq_sb[:],
                              start=True, stop=True).then_inc(sem["s_nmm"], 1)
                tensor.wait_ge(sem["s_khe"], idx + 1)
                tensor.wait_ge(sem["s_rk"], idx + 1)
                tensor.matmul(p1b_ps[0:64, :], oblk_sb[:, 0:64], sqk_sb[:],
                              start=True, stop=True).then_inc(sem["s_nmm"], 1)
            for j in range(4):
                tt = 4 * gp + j
                vp = p1a_ps if j % 2 == 0 else p1b_ps
                if j < 2:
                    tensor.wait_ge(sem["s_nmax"], 8 * gp + 7 + j)
                    if j == 1:
                        tensor.wait_ge(sem["s_bias"], gp + 1)
                else:
                    tensor.wait_ge(sem["s_vev"], tt - 1)
                for f in range(8):
                    ins = tensor.matmul(vp[:], utg_sb[:, f, j * 128:(j + 1) * 128],
                                        wvr_sb[:, f, :], start=(f == 0), stop=(f == 7))
                ins.then_inc(sem["s_vmm"], 1)

        def emit_pv(u):
            g, hp, kt = UNITS[u]
            first, last = (kt == 0), (kt == 4 * g + 3)
            diag = kt >= 4 * g
            q0 = 128 * (kt - 4 * g) if diag else 0
            pa, pb = p_sb[:, u % 2, q0:512], p_sb[:, u % 2, 512 + q0:1024]
            tensor.wait_ge(sem["s_exp"], u + 1)
            if first:
                tensor.wait_ge(sem["s_onorm"], 4 * g + hp + 1)
            tensor.matmul(o_psA[0:96, q0:512], vau[:, hp, kt, 0:96], pa,
                          start=first, stop=last, skip_group_check=True)
            tensor.matmul(o_psB[0:96, q0:512], vau[:, hp, kt, 96:192], pb,
                          start=first, stop=last, skip_group_check=True).then_inc(sem["s_pv"], 1)

        prep_pe(0)
        for g in range(NG):
            gc2 = (g % 2)
            pend = None
            u = UOFF[g]
            for hp in range(4):
                for kt in range(4 * g + 4):
                    if u >= 2:
                        tensor.wait_ge(sem["s_exp"], u - 1)
                    if hp == 0 and kt == 0:
                        if g > 0:
                            tensor.wait_ge(sem["s_paev"], 4 * g)
                        tensor.wait_ge(sem["s_qhe"], 4 * (g + 1))
                        tensor.wait_ge(sem["s_khe"], 4 * (g + 1))
                        tensor.wait_ge(sem["s_vev"], 4 * (g + 1))
                    kc = slice(kt * 128, (kt + 1) * 128)
                    diag = kt >= 4 * g
                    q0 = 128 * (kt - 4 * g) if diag else 0
                    tensor.matmul(s_pair[u % 2][:, q0:512], khat[0:64, hp, kc],
                                  qhat[0:64, gc2, hp, q0:512], start=True, stop=not diag,
                                  skip_group_check=True)
                    ins2 = tensor.matmul(s_pair[u % 2][:, 512 + q0:1024], khat[64:128, hp, kc],
                                         qhat[64:128, gc2, hp, q0:512], start=True, stop=not diag,
                                         skip_group_check=True)
                    if diag:
                        tensor.matmul(s_pair[u % 2][:, q0:q0 + 128], idn_sb[:],
                                      msk_sb[:], start=False, stop=True, skip_group_check=True)
                        tensor.matmul(s_pair[u % 2][:, 512 + q0:512 + q0 + 128], idn_sb[:],
                                      msk_sb[:], start=False, stop=True,
                                      skip_group_check=True).then_inc(sem["s_qk2"], 1)
                    else:
                        ins2.then_inc(sem["s_qk2"], 1)
                    if pend is not None:
                        emit_pv(pend)
                    pend = u
                    u += 1
            emit_pv(pend)
            if g < 7:
                prep_pe(g + 1)
            for j in range(4):
                tt = 4 * g + j
                if j == 0:
                    tensor.wait_ge(sem["s_onorm"], 4 * (g + 1) + 1)
                if tt >= 2:
                    tensor.wait_ge(sem["s_paev"], tt - 1)
                wo_ps = s_pair[j % 2]
                for half in range(2):
                    for hp in range(4):
                        ins = tensor.matmul(wo_ps[:, half * 512:(half + 1) * 512],
                                            onog[:, hp, j * 128:(j + 1) * 128],
                                            wo_bf[:, hp, half * 512:(half + 1) * 512],
                                            start=(hp == 0), stop=(hp == 3), skip_group_check=True)
                ins.then_inc(sem["s_womm"], 1)

    return nc, ctx


_PROG = None


def _rope_tables():
    inv_freq = (1.0 / 10000.0 ** (np.arange(0, DK, 2, dtype=np.float32) / np.float32(DK))).astype(np.float32)
    pos = np.arange(S, dtype=np.float32)
    fr = (pos[:, None] * inv_freq[None, :]).astype(np.float32)
    c = np.cos(fr.astype(np.float64)).astype(np.float32).T
    s = np.sin(fr.astype(np.float64)).astype(np.float32).T
    t1 = np.ascontiguousarray(np.concatenate([c, c, -s, -s], axis=0))
    t2 = np.ascontiguousarray(np.concatenate([s, s, c, c], axis=0))
    return t1, t2


def kernel(**inputs):
    global _PROG
    if _PROG is None:
        _PROG = _build()
    nc, _ctx = _PROG

    x = np.asarray(inputs["L0_attn_x"], np.float32)
    Wq = np.asarray(inputs["L0_attn_Wq"], np.float32)
    Wk = np.asarray(inputs["L0_attn_Wk"], np.float32)
    Wv = np.asarray(inputs["L0_attn_Wv"], np.float32)
    Wo = np.asarray(inputs["L0_attn_Wo"], np.float32)
    bq = np.asarray(inputs["L0_attn_bq"], np.float32)
    bk = np.asarray(inputs["L0_attn_bk"], np.float32)
    bv = np.asarray(inputs["L0_attn_bv"], np.float32)
    bo = np.asarray(inputs["L0_attn_bo"], np.float32)
    ln_g = np.asarray(inputs["L0_ffn_ln_g"], np.float32)
    ln_b = np.asarray(inputs["L0_ffn_ln_b"], np.float32)

    t1, t2 = _rope_tables()
    kk = np.arange(128)[:, None]
    qq = np.arange(GT)[None, :]
    kk128 = np.arange(128)[:, None]
    qq128 = np.arange(128)[None, :]
    msk = np.where(kk128 <= qq128, np.float32(0), np.float32(-1e9)).astype(BF)
    idn = np.eye(128, dtype=np.float32).astype(BF)
    oblk = np.zeros((128, 64), np.float32)
    oblk[0:64, 0:32] = 1.0
    oblk[64:128, 32:64] = 1.0
    oblk = oblk.astype(BF)
    map1 = np.concatenate([np.arange(32), np.arange(32, 64) + 32,
                           np.arange(64, 96) - 64, np.arange(96, 128) - 32])
    map2 = np.concatenate([np.arange(32) + 32, np.arange(32, 64) + 64,
                           np.arange(64, 96) - 32, np.arange(96, 128)])
    a1 = np.zeros((128, 128), np.float32)
    a1[np.arange(128), map1] = 1.0
    a2 = np.zeros((128, 128), np.float32)
    a2[np.arange(128), map2] = 1.0
    a1 = a1.astype(BF)
    a2 = a2.astype(BF)
    g_bc = np.ascontiguousarray(np.broadcast_to(ln_g, (128, D)), dtype=np.float32)
    b_bc = np.ascontiguousarray(np.broadcast_to(ln_b, (128, D)), dtype=np.float32)
    bo_bc = np.ascontiguousarray(np.broadcast_to(bo, (128, D)), dtype=np.float32)

    in_maps = []
    for c in range(8):
        b, hg = c // 2, c % 2
        heads = [hg * 8 + i for i in range(8)]
        perm = []
        for hp in range(4):
            a, b2 = heads[2 * hp], heads[2 * hp + 1]
            perm += (list(a * 64 + np.arange(0, 64, 2)) + list(b2 * 64 + np.arange(0, 64, 2))
                     + list(a * 64 + np.arange(1, 64, 2)) + list(b2 * 64 + np.arange(1, 64, 2)))
        perm = np.asarray(perm)
        vcols = np.arange(hg * 512, hg * 512 + 512)
        worows = []
        for hp in range(4):
            a, b2 = heads[2 * hp], heads[2 * hp + 1]
            worows += list(a * 64 + np.arange(64)) + list(b2 * 64 + np.arange(64))
        worows = np.asarray(worows)
        own_rows = np.concatenate([512 * gg + 256 * hg + np.arange(256) for gg in range(8)])
        in_maps.append({
            "x_b": np.ascontiguousarray(x[b]),
            "x_own": np.ascontiguousarray(x[b][own_rows]),
            "wq_d": np.ascontiguousarray(Wq[:, perm]).reshape(8, 128, 512),
            "wk_d": np.ascontiguousarray(Wk[:, perm]).reshape(8, 128, 512),
            "wv_d": np.ascontiguousarray(Wv[:, vcols]).reshape(8, 128, 512),
            "wo_d": np.ascontiguousarray(Wo[worows, :]).reshape(4, 128, 1024),
            "bq_d": np.ascontiguousarray(bq[perm].reshape(4, 128).T),
            "bk_d": np.ascontiguousarray(bk[perm].reshape(4, 128).T),
            "bv_d": np.ascontiguousarray(np.broadcast_to(bv[vcols], (128, 512)), dtype=np.float32),
            "t1_d": t1, "t2_d": t2, "msk_d": np.ascontiguousarray(msk),
            "idn_d": idn, "oblk_d": oblk,
            "a1_d": a1, "a2_d": a2,
            "g_d": g_bc, "bb_d": b_bc, "bo_d": bo_bc,
        })

    res = run_bass_kernel_spmd(nc, in_maps, core_ids=list(range(8)))
    out = np.empty((B, S, D), np.float32)
    for c in range(8):
        b, hg = c // 2, c % 2
        o = np.asarray(res.results[c]["out"]).reshape(8, 256, D)
        for gg in range(8):
            out[b, 512 * gg + 256 * hg: 512 * gg + 256 * (hg + 1)] = o[gg]
    return out
